# version 19
# baseline (speedup 1.0000x reference)
import numpy as np
import ml_dtypes
from contextlib import ExitStack

import concourse.bass as bass
import concourse.mybir as mybir
from concourse.bass_utils import run_bass_kernel_spmd

F32 = mybir.dt.float32
BF16 = mybir.dt.bfloat16
AF = mybir.ActivationFunctionType
ALU = mybir.AluOpType
AX = mybir.AxisListType

D = 1024
HF = 4096
NE = 16
NH = 16
DH = 64
NT = 8
NSET = 4
R2 = 32768
EPS = 1e-5

DO_FFN1 = True
DO_ATTN = True
DO_FFN2 = True
DEBUG_TAPS = False
ATTN_PART = "full"

ENGS = ("pe", "act", "dve", "pool", "sp")


def _cp(E, d, s):
    return E.tensor_copy(d, s) if hasattr(E, "tensor_copy") else E.copy(d, s)


def _scale(E, d, s, c):
    if hasattr(E, "tensor_scalar"):
        return E.tensor_scalar(d, s, c, None, ALU.mult)
    return E.mul(d, s, c)


class _Reg:
    __slots__ = ("writer", "readers", "name")

    def __init__(self, name=""):
        self.writer = None
        self.readers = []
        self.name = name


NDQ = 8


class _Op:
    __slots__ = ("eng", "fn", "deps", "rdeps", "pdeps", "need_inc", "inc_no",
                 "dma_val", "self_wait")

    def __init__(self, eng, fn):
        self.eng = eng
        self.fn = fn
        self.deps = []
        self.rdeps = []
        self.pdeps = []
        self.need_inc = False
        self.inc_no = None
        self.dma_val = None
        self.self_wait = None


class _KB:
    def __init__(self):
        self.eops = {e: [] for e in ENGS}
        self.pending = {e: [] for e in ENGS}
        self.dma_count = 0
        self.dq_uses = [0] * NDQ

    def op(self, eng, fn, reads=(), writes=(), ptr=()):
        o = _Op(eng, fn)
        for r in reads:
            if r.writer is not None:
                o.rdeps.append(r.writer)
        for r in ptr:
            if r.writer is not None:
                o.pdeps.append(r.writer)
        for w in writes:
            if w.writer is not None:
                o.deps.append(w.writer)
            o.deps.extend(w.readers)
        o.deps.extend(self.pending[eng])
        self.pending[eng] = []
        for r in reads:
            r.readers.append(o)
        for r in ptr:
            r.readers.append(o)
        for w in writes:
            w.writer = o
            w.readers = []
        self.eops[eng].append(o)
        return o

    def dma(self, fn, reads=(), writes=(), row=False):
        o = self.op("sp", fn, reads=reads, writes=writes)
        j = self.dma_count % NDQ
        self.dma_count += 1
        u = self.dq_uses[j]
        if u > 0:
            o.self_wait = (f"dq{j}", 16 * u)
        self.dq_uses[j] = u + 1
        o.dma_val = (f"dq{j}", 16 * (u + 1))
        return o

    def barrier(self):
        last = {e: (self.eops[e][-1] if self.eops[e] else None) for e in ENGS}
        for e in ENGS:
            deps = [last[o] for o in ENGS if o != e and last[o] is not None]
            self.pending[e].extend(deps)

    def finalize(self):
        for e in ENGS:
            for o in self.eops[e]:
                for d in o.deps + o.rdeps + o.pdeps:
                    if d.dma_val is None and not (d.eng == o.eng == "pe"):
                        d.need_inc = True
        for e in ENGS:
            n = 0
            for o in self.eops[e]:
                if o.need_inc:
                    n += 1
                    o.inc_no = n

    def emit(self, eng_name, E, sems):
        waited = {}
        my_sem = sems.get(eng_name)
        for o in self.eops[eng_name]:
            wl = {}
            for d in o.deps + o.rdeps + o.pdeps:
                if d.dma_val is not None:
                    sname, val = d.dma_val
                    wl[sname] = max(wl.get(sname, 0), val)
                else:
                    if d.eng == eng_name == "pe":
                        continue
                    assert d.inc_no is not None
                    wl[d.eng] = max(wl.get(d.eng, 0), d.inc_no)
            if o.self_wait is not None:
                sname, val = o.self_wait
                wl[sname] = max(wl.get(sname, 0), val)
            for sname, val in wl.items():
                if waited.get(sname, 0) >= val:
                    continue
                E.wait_ge(sems[sname], val)
                waited[sname] = val
            inst = o.fn(E)
            if o.dma_val is not None:
                inst.then_inc(sems[o.dma_val[0]], 16)
            elif o.need_inc:
                inst.then_inc(my_sem, 1)


def _build():
    nc = bass.Bass("TRN2", target_bir_lowering=False, debug=False)
    kb = _KB()

    dx = nc.dram_tensor("x", [4096, D], F32, kind="ExternalInput").ap()
    dout = nc.dram_tensor("out", [4096, D], F32, kind="ExternalOutput").ap()
    dwt = {}
    for nm in ("wqt", "wkt", "wvt", "wot"):
        dwt[nm] = nc.dram_tensor(nm, [D, D], F32, kind="ExternalInput").ap()
    dw1t = nc.dram_tensor("w1t", [D, HF], F32, kind="ExternalInput").ap()
    dw2t = nc.dram_tensor("w2t", [HF, D], F32, kind="ExternalInput").ap()
    drw = nc.dram_tensor("rwin", [128, 8 * NE], F32, kind="ExternalInput").ap()
    dbias = {}
    for nm, n in (("b1", HF), ("b2", D), ("bq", D), ("bk", D), ("bv", D),
                  ("bo", D), ("rb", NE)):
        dbias[nm] = nc.dram_tensor(nm, [1, n], F32, kind="ExternalInput").ap()
    didb = nc.dram_tensor("idb", [128, 128], BF16, kind="ExternalInput").ap()
    didf = nc.dram_tensor("idf", [128, 128], F32, kind="ExternalInput").ap()
    dmask = nc.dram_tensor("maskf", [128, 128], F32, kind="ExternalInput").ap()
    donescol = nc.dram_tensor("onescol", [128, 1], BF16, kind="ExternalInput").ap()
    dscr_w2 = nc.dram_tensor("scr_w2", [128, R2], BF16, kind="Internal").ap()
    dscr_at = nc.dram_tensor("scr_at", [128, R2], BF16, kind="Internal").ap()
    if DEBUG_TAPS:
        ddbg = nc.dram_tensor("dbg", [128, 64], F32, kind="ExternalOutput").ap()
        ddbgb = nc.dram_tensor("dbgb", [128, 2048], BF16,
                               kind="ExternalOutput").ap()

    ctx = ExitStack()
    sb = lambda n, s, dt: ctx.enter_context(nc.sbuf_tensor(n, s, dt))
    psa = lambda n, s, dt: nc.psum_tensor(n, s, dt)

    X = sb("X", [128, NT, D], F32)
    x_reg = [_Reg(f"x{t}") for t in range(NT)]
    arena = sb("arena", [128, 65536], BF16)
    w1_reg = [_Reg(f"w1_{k}") for k in range(8)]
    r2_reg = [_Reg(f"r2_{j}") for j in range(4)]
    kv_sb = sb("kv_sb", [128, 520], F32)
    kv_bf = [sb("kv_bf0", [128, 520], BF16), sb("kv_bf1", [128, 520], BF16)]
    kv_sb_r = _Reg("kv_sb")
    kv_bf_r = [_Reg("kv_bf0"), _Reg("kv_bf1")]
    sums = [sb(f"sums{i}", [128, NT], F32) for i in range(3)]
    sums_r = [_Reg(f"sums{i}") for i in range(3)]
    ssq = sb("ssq", [128, NT], F32)
    ssq_r = _Reg("ssq")
    sqj_g = sb("sqj_g", [128, 1024], F32)
    sqj_g_r = _Reg("sqj_g")
    mtmp = sb("mtmp", [128, NT], F32)
    vtmp = sb("vtmp", [128, NT], F32)
    ltmp = sb("ltmp", [128, NT], F32)
    rstd = sb("rstd", [128, NT], F32)
    nmr = sb("nmr", [128, NT], F32)
    st_r = {"m": _Reg("m"), "v": _Reg("v"), "l": _Reg("l"), "r": _Reg("r"), "n": _Reg("n")}
    rw = sb("rw_sb", [128, 8 * NE], F32)
    rw_r = _Reg("rw")
    brow = sb("brow", [128, HF + 128], BF16)
    brow_r = _Reg("brow")
    frow = sb("frow", [128, 544], F32)
    frow_r = _Reg("frow")
    epscol = sb("epscol", [128, 1], F32)
    idb = sb("idb_sb", [128, 128], BF16)
    idf = sb("idf_sb", [128, 128], F32)
    maskf = sb("maskf_sb", [128, 128], F32)
    onescol = sb("onescol_sb", [128, 1], BF16)
    const_r = _Reg("const")

    b1row = lambda lo, n: brow[0:1, lo:lo + n]
    bqrow = lambda lo, n: brow[32:33, lo:lo + n]
    bkrow = lambda lo, n: brow[32:33, 1024 + lo:1024 + lo + n]
    bvrow = lambda lo, n: brow[32:33, 2048 + lo:2048 + lo + n]
    b2row = lambda lo, n: brow[32:33, 3072 + lo:3072 + lo + n]
    borow = lambda lo, n: brow[64:65, lo:lo + n]
    ones0 = lambda n: brow[0:1, HF:HF + n]
    ones32 = lambda n: brow[32:33, HF:HF + n]
    ones64 = lambda n: brow[64:65, HF:HF + n]
    onesf = lambda n: frow[0:1, 0:n]
    rbrow = frow[0:1, 512:512 + NE]

    def set_rows(s):
        return (s // 2) * 2048 + (s % 2) * 1024

    def w1v(kc, lo, n):
        return arena[:, kc * HF + lo: kc * HF + lo + n]

    def w2v(hc, lo, n):
        return arena[:, R2 + hc * D + lo: R2 + hc * D + lo + n]

    def atv(m, kc, lo, n):
        base = R2 + m * 8192 + kc * D
        return arena[:, base + lo: base + lo + n]

    def dmaop(dst, src, reads=(), writes=(), row=False, slow=False):
        return kb.dma(lambda E, d=dst, s=src, sl=slow: E.dma_start(
            out=d, in_=s, allow_slow_non_contiguous=sl),
            reads=reads, writes=writes, row=row)

    dmaop(idb[:], didb, writes=[const_r])
    dmaop(idf[:], didf, writes=[const_r])
    dmaop(maskf[:], dmask, writes=[const_r])
    dmaop(onescol[:], donescol, writes=[const_r])
    dmaop(rw[:], drw, writes=[rw_r])

    astg = sb("astg", [128, 1024], F32)
    astg_r = _Reg("astg")
    abounce = sb("abounce", [128, 1024], BF16)
    abounce_r = _Reg("abounce")
    scr_at_r = _Reg("scr_at")

    p0ctx = ExitStack()
    stg = [p0ctx.enter_context(nc.sbuf_tensor(f"stg{i}", [128, 2048], F32))
           for i in range(2)]
    stg_r = [_Reg(f"stg{i}") for i in range(2)]

    cast_engine_rr = ["dve", "act", "pool"]

    def load_bias(name, dstap, n, scale=None):
        srcs = dbias[name]
        for piece in range(0, n, 2048):
            w = min(2048, n - piece)
            dmaop(stg[0][0:1, 0:w], srcs[0:1, piece:piece + w],
                  writes=[stg_r[0]], row=True)
            if scale is None:
                kb.op("dve", lambda E, d=dstap(piece, w), s=stg[0][0:1, 0:w]:
                      E.tensor_copy(d, s), reads=[stg_r[0]], writes=[brow_r])
            else:
                kb.op("dve", lambda E, d=dstap(piece, w), s=stg[0][0:1, 0:w],
                      sc=scale: E.tensor_scalar(d, s, sc, None, ALU.mult),
                      reads=[stg_r[0]], writes=[brow_r])

    load_bias("b1", b1row, HF)
    load_bias("bq", bqrow, D)
    load_bias("bk", bkrow, D)
    load_bias("bv", bvrow, D)
    load_bias("b2", b2row, D, scale=0.5)
    load_bias("bo", borow, D)
    for _ob in (ones0, ones32, ones64):
        kb.op("dve", lambda E, o=_ob(128): E.memset(o, 1.0), writes=[brow_r])
    kb.op("dve", lambda E: E.memset(epscol[:], EPS), writes=[const_r])
    kb.op("dve", lambda E: E.memset(onesf(512), 1.0), writes=[frow_r])
    dmaop(rbrow, dbias["rb"], writes=[frow_r], row=True)

    ce = 0
    for kc in range(8):
        for h2 in range(2):
            dst = stg[(kc * 2 + h2) % 2]
            dr = stg_r[(kc * 2 + h2) % 2]
            dmaop(dst[:], dw1t[kc * 128:(kc + 1) * 128, h2 * 2048:(h2 + 1) * 2048],
                  writes=[dr])
            eng = cast_engine_rr[ce % 3]; ce += 1
            kb.op(eng, lambda E, d=w1v(kc, h2 * 2048, 2048), s=dst[:]:
                  _cp(E, d, s), reads=[dr], writes=[w1_reg[kc]])
    for kc in range(8):
        for h2 in range(2):
            dst = stg[(kc * 2 + h2) % 2]
            dr = stg_r[(kc * 2 + h2) % 2]
            dmaop(dst[:].rearrange("p (b d) -> p b d", b=2),
                  dw2t[kc * 512 + h2 * 256:(kc * 512) + (h2 + 1) * 256, :]
                  .rearrange("(b a) d -> a b d", b=2),
                  writes=[dr])
            hc0 = kc * 4 + h2 * 2
            eng = ("dve", "act")[ce % 2]; ce += 1
            kb.op(eng, lambda E, d=w2v(hc0, 0, 2048), s=dst[:]:
                  _scale(E, d, s, 0.5),
                  reads=[dr], writes=[r2_reg[hc0 // 8]])

    scr_w2_r = _Reg("scr_w2")

    def emit_w2_spill():
        for j in range(4):
            dmaop(dscr_w2[:, j * 8192:(j + 1) * 8192],
                  arena[:, R2 + j * 8192: R2 + (j + 1) * 8192],
                  reads=[r2_reg[j]], writes=[scr_w2_r])

    def emit_attn_cast_to_scratch():
        for m, nm in enumerate(("wqt", "wkt", "wvt", "wot")):
            for kc in range(8):
                dmaop(astg[:], dwt[nm][kc * 128:(kc + 1) * 128, :],
                      writes=[astg_r])
                kb.op("pool", lambda E: _cp(E, abounce[:], astg[:]),
                      reads=[astg_r], writes=[abounce_r])
                dmaop(dscr_at[:, m * 8192 + kc * 1024: m * 8192 + (kc + 1) * 1024],
                      abounce[:], reads=[abounce_r], writes=[scr_at_r])

    def emit_attn_prep():
        for m in range(4):
            dmaop(arena[:, R2 + m * 8192:R2 + (m + 1) * 8192],
                  dscr_at[:, m * 8192:(m + 1) * 8192],
                  reads=[scr_at_r], writes=[r2_reg[m]])

    def emit_w2_reload():
        for j in range(4):
            dmaop(arena[:, R2 + j * 8192:R2 + (j + 1) * 8192],
                  dscr_w2[:, j * 8192:(j + 1) * 8192],
                  reads=[scr_w2_r], writes=[r2_reg[j]])

    def emit_sq(t):
        kb.op("act", lambda E, t=t: E.activation(
            sqj_g[:], X[:, t, :], AF.Square, accum_out=ssq[:, t:t + 1]),
            reads=[x_reg[t]], writes=[ssq_r, sqj_g_r])

    def load_tile_dma(s, t):
        r0 = set_rows(s)
        dmaop(X[:, t, :], dx[r0 + t * 128: r0 + (t + 1) * 128, :],
              writes=[x_reg[t]])

    def load_tile_stats(t):
        kb.op("dve", lambda E, t=t: E.reduce_sum(
            sums[0][:, t:t + 1], X[:, t, :], axis=AX.X),
            reads=[x_reg[t]], writes=[sums_r[0]])
        emit_sq(t)

    def load_set(s):
        for t in range(NT):
            load_tile_dma(s, t)
        for t in range(NT):
            load_tile_stats(t)

    load_set(0)
    p0ctx.close()

    def stats_chain(sum_reg, sums_t):
        kb.op("dve", lambda E: E.tensor_scalar(
            mtmp[:], sums_t[:], 1.0 / D, None, ALU.mult),
            reads=[sum_reg], writes=[st_r["m"]])
        kb.op("dve", lambda E: E.scalar_tensor_tensor(
            vtmp[:], mtmp[:], -1.0, mtmp[:], ALU.mult, ALU.mult),
            reads=[st_r["m"]], writes=[st_r["v"]])
        kb.op("dve", lambda E: E.tensor_scalar(
            ltmp[:], ssq[:], 1.0 / D, EPS, ALU.mult, ALU.add),
            reads=[ssq_r], writes=[st_r["l"]])
        kb.op("dve", lambda E: E.tensor_tensor(
            vtmp[:], ltmp[:], vtmp[:], ALU.add),
            reads=[st_r["l"], st_r["v"]], writes=[st_r["v"]])
        kb.op("act", lambda E: E.activation(
            ltmp[:], vtmp[:], AF.Ln, bias=0.0, scale=1.0),
            reads=[st_r["v"]], writes=[st_r["l"]])
        kb.op("act", lambda E: E.activation(
            rstd[:], ltmp[:], AF.Exp, bias=0.0, scale=-0.5),
            reads=[st_r["l"]], writes=[st_r["r"]])
        kb.op("dve", lambda E: E.scalar_tensor_tensor(
            nmr[:], mtmp[:], -1.0, rstd[:], ALU.mult, ALU.mult),
            reads=[st_r["m"], st_r["r"]], writes=[st_r["n"]])


    def emit_apply(t, lnb_tm, lnb_r, f32_dst=None, f32_reg=None):
        kb.op("dve", lambda E, t=t: E.tensor_scalar(
            lnb_tm[:], X[:, t, :], rstd[:, t:t + 1], nmr[:, t:t + 1],
            ALU.mult, ALU.add),
            reads=[x_reg[t]], writes=[lnb_r], ptr=[st_r["r"], st_r["n"]])
        if f32_dst is not None:
            kb.op("dve", lambda E, t=t: E.tensor_scalar(
                f32_dst[:], X[:, t, :], rstd[:, t:t + 1], nmr[:, t:t + 1],
                ALU.mult, ALU.add),
                reads=[x_reg[t]], writes=[f32_reg], ptr=[st_r["r"], st_r["n"]])

    def transpose8(src, src_reg, pdst, pdst_reg, ident, n=8):
        for j in range(n):
            kb.op("pe", lambda E, j=j: E.transpose(
                pdst[:, j * 128:(j + 1) * 128], src[:, j * 128:(j + 1) * 128],
                ident[:]),
                reads=[src_reg, const_r], writes=[pdst_reg])

    def emit_ffn_set(P, s, role, sums_idx, out_sums_idx, after_tile=None,
                     pre_tail=None):
        stats_chain(sums_r[sums_idx], sums[sums_idx])

        def tbf_block(t):
            par = t % 2
            transpose8(P["lnb_tm"][par], P["lnb_tm_r"][par], P["pTbf"],
                       P["pTbf_r"], idb)
            kb.op("dve", lambda E, par=par: E.tensor_copy(
                P["lnb_fm"][par][:], P["pTbf"][:]),
                reads=[P["pTbf_r"]], writes=[P["lnb_fm_r"][par]])

        def router_a(t):
            transpose8(P["lnf_tm"], P["lnf_tm_r"], P["pTf32"], P["pTf32_r"], idf)
            kb.op("act", lambda E: E.copy(P["lnf_fm"][:], P["pTf32"][:]),
                  reads=[P["pTf32_r"]], writes=[P["lnf_fm_r"]])

        def router_b(t):
            par = t % 2
            kb.op("pe", lambda E: E.matmul(
                P["pTf32"][:, 0:NE], onesf(128), rbrow, start=True, stop=False),
                reads=[frow_r], writes=[P["pTf32_r"]])
            for kc in range(8):
                kb.op("pe", lambda E, kc=kc: E.matmul(
                    P["pTf32"][:, 0:NE],
                    P["lnf_fm"][:, kc * 128:(kc + 1) * 128],
                    rw[:, kc * NE:(kc + 1) * NE],
                    start=False, stop=(kc == 7)),
                    reads=[P["lnf_fm_r"], rw_r], writes=[P["pTf32_r"]])
            kb.op("dve", lambda E, par=par: E.tensor_copy(
                P["logits"][par][:], P["pTf32"][:, 0:NE]),
                reads=[P["pTf32_r"]], writes=[P["logits_r"][par]])
            kb.op("dve", lambda E, par=par: E.max(
                out=P["gmax"][par][:], in_=P["logits"][par][:]),
                reads=[P["logits_r"][par]], writes=[P["gmax_r"][par]])
            kb.op("dve", lambda E, par=par: E.tensor_scalar(
                P["g"][par][:], P["logits"][par][:],
                P["gmax"][par][:, 3:4], None, ALU.is_ge),
                reads=[P["logits_r"][par]], writes=[P["g_r"][par]],
                ptr=[P["gmax_r"][par]])

        def router_block(t):
            router_a(t)
            router_b(t)

        emit_apply(0, P["lnb_tm"][0], P["lnb_tm_r"][0], P["lnf_tm"],
                   P["lnf_tm_r"])
        router_block(0)
        tbf_block(0)
        for t in range(NT):
            par = t % 2
            if t + 1 < NT:
                emit_apply(t + 1, P["lnb_tm"][(t + 1) % 2],
                           P["lnb_tm_r"][(t + 1) % 2], P["lnf_tm"],
                           P["lnf_tm_r"])
            for half in range(2):
                kb.op("pe", lambda E, half=half: E.matmul(
                    P["pY"][:, half * 512:(half + 1) * 512],
                    ones32(128), b2row(half * 512, 512),
                    start=True, stop=False),
                    reads=[brow_r], writes=[P["pY_r"]])

            def h_group(oc):
                b = oc % 2
                kb.op("pe", lambda E, oc=oc, b=b: E.matmul(
                    P["pH"][:, b * 512:(b + 1) * 512],
                    ones0(128), b1row(oc * 512, 512), start=True, stop=False),
                    reads=[brow_r], writes=[P["pH_r"][b]])
                for kc in range(8):
                    kb.op("pe", lambda E, oc=oc, b=b, kc=kc, par=par: E.matmul(
                        P["pH"][:, b * 512:(b + 1) * 512],
                        P["lnb_fm"][par][:, kc * 128:(kc + 1) * 128],
                        w1v(kc, oc * 512, 512),
                        start=False, stop=(kc == 7)),
                        reads=[P["lnb_fm_r"][par], w1_reg[kc], brow_r],
                        writes=[P["pH_r"][b]])

            def gate(oc):
                b = oc % 2
                gb = P["g"][par][:, 2 * oc:2 * oc + 2].unsqueeze(2).to_broadcast(
                    [128, 2, 256])
                kb.op("dve", lambda E, oc=oc, b=b, gb=gb: E.scalar_tensor_tensor(
                    P["hg_tm"][b][:].rearrange("p (a c) -> p a c", a=2),
                    P["pH"][:, b * 512:(b + 1) * 512].rearrange(
                        "p (a c) -> p a c", a=2),
                    0.0, gb, ALU.max, ALU.mult),
                    reads=[P["pH_r"][b], P["g_r"][par]],
                    writes=[P["hg_tm_r"][b]])

            def hgT(oc):
                b = oc % 2
                for j in range(4):
                    kb.op("pe", lambda E, b=b, j=j: E.transpose(
                        P["pHT"][:, b * 512 + j * 128: b * 512 + (j + 1) * 128],
                        P["hg_tm"][b][:, j * 128:(j + 1) * 128], idb[:]),
                        reads=[P["hg_tm_r"][b], const_r], writes=[P["pHT_r"][b]])
                kb.op("act", lambda E, b=b: E.copy(
                    P["hg_fm"][b][:], P["pHT"][:, b * 512:(b + 1) * 512]),
                    reads=[P["pHT_r"][b]], writes=[P["hg_fm_r"][b]])

            def y_mms(oc):
                b = oc % 2
                for j in range(4):
                    hc = oc * 4 + j
                    for half in range(2):
                        kb.op("pe", lambda E, b=b, j=j, hc=hc, half=half, oc=oc:
                              E.matmul(
                                  P["pY"][:, half * 512:(half + 1) * 512],
                                  P["hg_fm"][b][:, j * 128:(j + 1) * 128],
                                  w2v(hc, half * 512, 512),
                                  start=False,
                                  stop=(oc == 7 and j == 3)),
                              reads=[P["hg_fm_r"][b], r2_reg[hc // 8]],
                              writes=[P["pY_r"]])

            h_group(0)
            h_group(1)
            for oc in range(8):
                gate(oc)
                hgT(oc)
                if oc == 1 and t + 1 < NT:
                    router_a(t + 1)
                if oc == 3 and t + 1 < NT:
                    router_b(t + 1)
                if oc == 4 and t + 1 < NT:
                    tbf_block(t + 1)
                if oc + 2 < 8:
                    h_group(oc + 2)
                if oc >= 1:
                    y_mms(oc - 1)
            y_mms(7)
            if pre_tail is not None and t == NT - 1:
                pre_tail()
            if out_sums_idx is not None:
                kb.op("dve", lambda E, t=t: E.scalar_tensor_tensor(
                    X[:, t, :], P["pY"][:], 1.0, X[:, t, :], ALU.mult, ALU.add,
                    accum_out=sums[out_sums_idx][:, t:t + 1]),
                    reads=[P["pY_r"], x_reg[t]],
                    writes=[x_reg[t], sums_r[out_sums_idx]])
                emit_sq(t)
            else:
                kb.op("dve", lambda E, t=t: E.scalar_tensor_tensor(
                    X[:, t, :], P["pY"][:], 1.0, X[:, t, :], ALU.mult, ALU.add),
                    reads=[P["pY_r"], x_reg[t]], writes=[x_reg[t]])
            if role == "f2":
                r0 = set_rows(s)
                dmaop(dout[r0 + t * 128:r0 + (t + 1) * 128, :], X[:, t, :],
                      reads=[x_reg[t]])
            if after_tile is not None:
                after_tile(t)

    def emit_out_only(s):
        r0 = set_rows(s)
        for t in range(NT):
            dmaop(dout[r0 + t * 128:r0 + (t + 1) * 128, :], X[:, t, :],
                  reads=[x_reg[t]])

    def ffn_phase(pi, f2_set, f1_set, f2_sums, f1_out_sums, f1_load=True,
                  prep_attn=None):
        with ExitStack() as ph:
            psb = lambda n, s, dt: ph.enter_context(
                nc.sbuf_tensor(f"{n}_p{pi}", s, dt))
            pps = lambda n, s, dt: ph.enter_context(
                nc.psum_tensor(f"{n}_p{pi}", s, dt))
            P = {}
            P["pTbf"] = pps("pTbf", [128, 1024], BF16); P["pTbf_r"] = _Reg()
            P["pTf32"] = pps("pTf32", [128, 1024], F32); P["pTf32_r"] = _Reg()
            P["pH"] = pps("pH", [128, 1024], F32)
            P["pH_r"] = [_Reg(), _Reg()]
            P["pY"] = pps("pY", [128, 1024], F32); P["pY_r"] = _Reg()
            P["pHT"] = pps("pHT", [128, 1024], BF16)
            P["pHT_r"] = [_Reg(), _Reg()]
            P["lnb_tm"] = [psb(f"lnbtm{i}", [128, 1024], BF16) for i in range(2)]
            P["lnb_tm_r"] = [_Reg(), _Reg()]
            P["lnb_fm"] = [psb(f"lnbfm{i}", [128, 1024], BF16) for i in range(2)]
            P["lnb_fm_r"] = [_Reg(), _Reg()]
            P["lnf_tm"] = psb("lnftm", [128, 1024], F32); P["lnf_tm_r"] = _Reg()
            P["lnf_fm"] = psb("lnffm", [128, 1024], F32); P["lnf_fm_r"] = _Reg()
            P["logits"] = [psb(f"lg{i}", [128, NE], F32) for i in range(2)]
            P["logits_r"] = [_Reg(), _Reg()]
            P["gmax"] = [psb(f"gm{i}", [128, 8], F32) for i in range(2)]
            P["gmax_r"] = [_Reg(), _Reg()]
            P["g"] = [psb(f"g{i}", [128, NE], F32) for i in range(2)]
            P["g_r"] = [_Reg(), _Reg()]
            P["hg_tm"] = [psb(f"hgtm{i}", [128, 512], BF16) for i in range(2)]
            P["hg_tm_r"] = [_Reg(), _Reg()]
            P["hg_fm"] = [psb(f"hgfm{i}", [128, 512], BF16) for i in range(2)]
            P["hg_fm_r"] = [_Reg(), _Reg()]

            pt = None
            if prep_attn is not None and DO_ATTN:
                pt = emit_attn_prep
            cb = None
            if f1_set is not None and f1_load:
                def cb(t, fs=f1_set):
                    load_tile_dma(fs, t)
                    if t >= 2:
                        load_tile_stats(t - 2)
                    if t == NT - 1:
                        load_tile_stats(NT - 2)
                        load_tile_stats(NT - 1)
            last_is_f1 = f1_set is not None and DO_FFN1
            if f2_set is not None:
                if DO_FFN2:
                    emit_ffn_set(P, f2_set, "f2", f2_sums, None, after_tile=cb,
                                 pre_tail=None if last_is_f1 else pt)
                    cb = None
                else:
                    emit_out_only(f2_set)
                    if not last_is_f1 and pt is not None:
                        pt()
            if f1_set is not None:
                if cb is not None:
                    load_set(f1_set)
                if DO_FFN1:
                    emit_ffn_set(P, f1_set, "f1", 0, f1_out_sums, pre_tail=pt)
        kb.barrier()

    def attn_phase(pi, s, first, reset):
        with ExitStack() as ph:
            psb = lambda n, sh, dt: ph.enter_context(
                nc.sbuf_tensor(f"{n}_a{pi}", sh, dt))
            pps = lambda n, sh, dt: ph.enter_context(
                nc.psum_tensor(f"{n}_a{pi}", sh, dt))
            pQ = pps("pQ", [128, 1024], F32)
            pQ_r = [_Reg("pQ0"), _Reg("pQ1")]
            pK = pps("pK", [128, 1024], F32)
            pK_r = [_Reg("pK0"), _Reg("pK1")]
            pV = pps("pV", [128, 1024], F32)
            pV_r = [_Reg("pV0"), _Reg("pV1")]
            pT1 = pps("pT1", [128, 1024], BF16); pT1_r = _Reg("pT1")
            pT2 = pps("pT2", [128, 1024], BF16); pT2_r = _Reg("pT2")
            lnb_tm = psb("lnbtm", [128, 1024], BF16); lnb_tm_r = _Reg()
            lnb_fm = psb("lnbfm", [128, 1024], BF16); lnb_fm_r = _Reg()
            scr, scr_r = sqj_g, sqj_g_r
            q_sp = psb("qsp", [128, 1024], BF16); q_sp_r = _Reg()
            k_sp = psb("ksp", [128, 1024], BF16); k_sp_r = _Reg()
            q_fm = psb("qfm", [128, 1024], BF16); q_fm_r = _Reg()
            k_fm = psb("kfm", [128, 1024], BF16); k_fm_r = _Reg()
            v_tm = psb("vtm", [128, 1024], BF16); v_tm_r = _Reg()
            am = [psb(f"am{i}", [128, 512], BF16) for i in range(2)]
            am_r = [_Reg(), _Reg()]
            att_tm = psb("atm", [128, 1024], BF16)
            att_tm_r = [_Reg(), _Reg()]
            att_fm = psb("afm", [128, 1024], BF16)
            att_fm_r = [_Reg(), _Reg()]
            r_all = psb("rall", [128, NE], F32); r_all_r = _Reg()

            if reset:
                kb.op("dve", lambda E: E.memset(kv_sb[:], 0.0), writes=[kv_sb_r])

            stats_chain(sums_r[1], sums[1])
            emit_apply(0, lnb_tm, lnb_tm_r)

            def ln2t_block():
                transpose8(lnb_tm, lnb_tm_r, pT1, pT1_r, idb)
                kb.op("dve", lambda E: E.tensor_copy(lnb_fm[:], pT1[:]),
                      reads=[pT1_r], writes=[lnb_fm_r])

            def proj_one(ps, ps_r, m, bias):
                for half in range(2):
                    kb.op("pe", lambda E, ps=ps, half=half, bias=bias:
                          E.matmul(ps[:, half * 512:(half + 1) * 512],
                                   ones32(128), bias(half * 512, 512),
                                   start=True, stop=False),
                          reads=[brow_r], writes=[ps_r[half]])
                    for kc in range(8):
                        kb.op("pe", lambda E, ps=ps, half=half, kc=kc, m=m:
                              E.matmul(ps[:, half * 512:(half + 1) * 512],
                                       lnb_fm[:, kc * 128:(kc + 1) * 128],
                                       atv(m, kc, half * 512, 512),
                                       start=False, stop=(kc == 7)),
                              reads=[lnb_fm_r, r2_reg[m], brow_r],
                              writes=[ps_r[half]])

            ln2t_block()
            proj_one(pQ, pQ_r, 0, bqrow)
            for c in range(NT):
                skip_cross = reset and c == 0
                for (ps, ps_r, m, bias) in ((pK, pK_r, 1, bkrow),
                                            (pV, pV_r, 2, bvrow)):
                    proj_one(ps, ps_r, m, bias)
                kb.op("act", lambda E: E.activation(scr[:], pQ[:], AF.Exp),
                      reads=[pQ_r[0], pQ_r[1]], writes=[scr_r])
                kb.op("act", lambda E: E.activation(
                    q_sp[:], scr[:], AF.Ln, bias=1.0),
                    reads=[scr_r], writes=[q_sp_r])
                kb.op("act", lambda E: E.activation(scr[:], pK[:], AF.Exp),
                      reads=[pK_r[0], pK_r[1]], writes=[scr_r])
                kb.op("act", lambda E: E.activation(
                    k_sp[:], scr[:], AF.Ln, bias=1.0),
                    reads=[scr_r], writes=[k_sp_r])
                kb.op("dve", lambda E: E.tensor_copy(v_tm[:], pV[:]),
                      reads=[pV_r[0], pV_r[1]], writes=[v_tm_r])
                if c + 1 < NT:
                    emit_apply(c + 1, lnb_tm, lnb_tm_r)
                if ATTN_PART == "qkv":
                    continue
                transpose8(q_sp, q_sp_r, pT1, pT1_r, idb)
                kb.op("dve", lambda E: E.tensor_copy(q_fm[:], pT1[:]),
                      reads=[pT1_r], writes=[q_fm_r])
                transpose8(k_sp, k_sp_r, pT2, pT2_r, idb)
                kb.op("dve", lambda E: E.tensor_copy(k_fm[:], pT2[:]),
                      reads=[pT2_r], writes=[k_fm_r])
                for h in range(NH):
                    kdst = pV[(h % 2) * 64:(h % 2) * 64 + 64,
                              (h // 2) * 64:(h // 2) * 64 + 64]
                    kb.op("pe", lambda E, h=h, kdst=kdst: E.matmul(
                        kdst, k_sp[:, h * 64:(h + 1) * 64],
                        v_tm[:, h * 64:(h + 1) * 64], start=True, stop=True),
                        reads=[k_sp_r, v_tm_r], writes=[pV_r[0]])
                    k1dst = pV[(h % 2) * 64:(h % 2) * 64 + 64,
                               512 + h // 2:513 + h // 2]
                    kb.op("pe", lambda E, h=h, k1dst=k1dst: E.matmul(
                        k1dst, k_sp[:, h * 64:(h + 1) * 64], onescol[:],
                        start=True, stop=True),
                        reads=[k_sp_r, const_r], writes=[pV_r[1]])

                def hsl(buf, h):
                    return buf[(h % 2) * 64:(h % 2) * 64 + 64,
                               (h // 2) * 128:(h // 2) * 128 + 128]

                def at_group(g):
                    gb2 = (g % 2) * 256
                    for j in range(4):
                        h = g * 4 + j
                        bank = h % 2
                        col = bank * 512 + gb2 + (j // 2) * 128
                        kb.op("pe", lambda E, h=h, col=col: E.matmul(
                            pQ[:, col:col + 128],
                            hsl(k_fm, h), hsl(q_fm, h), start=True, stop=True),
                            reads=[k_fm_r, q_fm_r], writes=[pQ_r[bank]])
                    mb = maskf[:].unsqueeze(1).to_broadcast([128, 2, 128])
                    amv = am[g % 2][:].rearrange("p (a b c) -> p a b c", a=2, b=2)
                    for par in range(2):
                        src = pQ[:, par * 512 + gb2: par * 512 + gb2 + 256]
                        kb.op("dve", lambda E, par=par, src=src, amv=amv,
                              mb=mb: E.tensor_tensor(
                            amv[:, :, par, :],
                            src.rearrange("p (a c) -> p a c", a=2),
                            mb, ALU.mult),
                            reads=[pQ_r[par], const_r], writes=[am_r[g % 2]])

                def numden(g):
                    for j in range(4):
                        h = g * 4 + j
                        amh = am[g % 2][:, j * 128:(j + 1) * 128]
                        nsl = pK[:, h * 64:(h + 1) * 64]
                        kb.op("pe", lambda E, amh=amh, nsl=nsl, h=h,
                              sc=skip_cross: E.matmul(
                            nsl, amh, v_tm[:, h * 64:(h + 1) * 64],
                            start=True, stop=sc),
                            reads=[am_r[g % 2], v_tm_r], writes=[pK_r[h // 8]])
                        if not skip_cross:
                            kvblk = kv_bf[c % 2][(h % 2) * 64:(h % 2) * 64 + 64,
                                                 (h // 2) * 64:(h // 2) * 64 + 64]
                            kb.op("pe", lambda E, nsl=nsl, h=h, kvblk=kvblk:
                                  E.matmul(nsl, hsl(q_fm, h), kvblk,
                                           start=False, stop=True),
                                  reads=[q_fm_r, kv_bf_r[c % 2]],
                                  writes=[pK_r[h // 8]])
                        dsl = pV[:, 528 + h:529 + h]
                        kb.op("pe", lambda E, amh=amh, dsl=dsl,
                              sc=skip_cross: E.matmul(
                            dsl, amh, onescol[:], start=True, stop=sc),
                            reads=[am_r[g % 2], const_r], writes=[pV_r[1]])
                        if not skip_cross:
                            k1c = kv_bf[c % 2][(h % 2) * 64:(h % 2) * 64 + 64,
                                               512 + h // 2:513 + h // 2]
                            kb.op("pe", lambda E, dsl=dsl, h=h, k1c=k1c:
                                  E.matmul(dsl, hsl(q_fm, h), k1c,
                                           start=False, stop=True),
                                  reads=[q_fm_r, kv_bf_r[c % 2]],
                                  writes=[pV_r[1]])

                if ATTN_PART == "trans":
                    continue
                if ATTN_PART == "atg":
                    at_group(0)
                    at_group(1)
                    at_group(2)
                    at_group(3)
                    continue
                at_group(0)
                at_group(1)
                numden(0)
                at_group(2)
                numden(1)
                at_group(3)
                numden(2)
                numden(3)
                if c + 1 < NT:
                    ln2t_block()
                    proj_one(pQ, pQ_r, 0, bqrow)
                if ATTN_PART == "numden":
                    continue
                kb.op("dve", lambda E: E.reciprocal(r_all[:], pV[:, 528:544]),
                      reads=[pV_r[1]], writes=[r_all_r])
                for hf in range(2):
                    rb_ = r_all[:, hf * 8:(hf + 1) * 8].unsqueeze(2)\
                        .to_broadcast([128, 8, DH])
                    sl = slice(hf * 512, (hf + 1) * 512)
                    kb.op("dve", lambda E, rb_=rb_, sl=sl: E.tensor_tensor(
                        att_tm[:, sl].rearrange("p (h e) -> p h e", h=8),
                        pK[:, sl].rearrange("p (h e) -> p h e", h=8),
                        rb_, ALU.mult),
                        reads=[pK_r[hf], r_all_r], writes=[att_tm_r[hf]])
                    for j in range(4):
                        jj = hf * 4 + j
                        kb.op("pe", lambda E, jj=jj: E.transpose(
                            pT2[:, jj * 128:(jj + 1) * 128],
                            att_tm[:, jj * 128:(jj + 1) * 128], idb[:]),
                            reads=[att_tm_r[hf], const_r], writes=[pT2_r])
                    kb.op("dve", lambda E, sl=sl: E.tensor_copy(
                        att_fm[:, sl], pT2[:, sl]),
                        reads=[pT2_r], writes=[att_fm_r[hf]])
                if ATTN_PART == "scan":
                    continue
                kb.op("dve", lambda E: E.tensor_tensor(
                    kv_sb[:], kv_sb[:], pV[:, 0:520], ALU.add),
                    reads=[pV_r[0], pV_r[1], kv_sb_r], writes=[kv_sb_r])
                kb.op("dve", lambda E, c=c: E.tensor_copy(
                    kv_bf[(c + 1) % 2][:], kv_sb[:]),
                    reads=[kv_sb_r], writes=[kv_bf_r[(c + 1) % 2]])
                for half in range(2):
                    kb.op("pe", lambda E, half=half: E.matmul(
                        pK[:, half * 512:(half + 1) * 512], ones64(128),
                        borow(half * 512, 512), start=True, stop=False),
                        reads=[brow_r], writes=[pK_r[half]])
                for kc in range(8):
                    for half in range(2):
                        kb.op("pe", lambda E, half=half, kc=kc: E.matmul(
                            pK[:, half * 512:(half + 1) * 512],
                            att_fm[:, kc * 128:(kc + 1) * 128],
                            atv(3, kc, half * 512, 512),
                            start=False, stop=(kc == 7)),
                            reads=[att_fm_r[kc // 4], r2_reg[3], brow_r],
                            writes=[pK_r[half]])
                kb.op("dve", lambda E, c=c: E.scalar_tensor_tensor(
                    X[:, c, :], pK[:], 1.0, X[:, c, :], ALU.mult, ALU.add,
                    accum_out=sums[2][:, c:c + 1]),
                    reads=[pK_r[0], pK_r[1], x_reg[c]],
                    writes=[x_reg[c], sums_r[2]])
                emit_sq(c)
                if ATTN_PART == "proj":
                    continue
            emit_w2_reload()
        kb.barrier()

    kb.barrier()
    emit_w2_spill()
    if DO_ATTN:
        emit_attn_cast_to_scratch()
    ffn_phase(1, None, 0, None, 1, f1_load=False, prep_attn="reload")
    for s in range(NSET):
        if DO_ATTN:
            attn_phase(2 + 2 * s, s, first=(s == 0), reset=(s % 2 == 0))
        f2_sums = 2 if DO_ATTN else 1
        if not DO_FFN1:
            f2_sums = 2 if DO_ATTN else 0
        ffn_phase(3 + 2 * s, s, s + 1 if s + 1 < NSET else None, f2_sums, 1,
                  prep_attn=("reload" if s + 1 < NSET else None))

    for j in range(NDQ):
        kb.op("sp", lambda E, j=j: E.wait_ge(_SEMS[f"dq{j}"],
                                             16 * kb.dq_uses[j]))

    kb.finalize()
    global _SEMS
    sem_names = ["pe", "act", "dve", "pool", "sp"] + [f"dq{j}" for j in range(NDQ)]
    sem_ctx = [ctx.enter_context(nc.semaphore(n)) for n in sem_names]
    _SEMS = dict(zip(sem_names, sem_ctx))
    with nc.Block() as block:
        @block.tensor
        def _(E):
            kb.emit("pe", E, _SEMS)

        @block.scalar
        def _(E):
            kb.emit("act", E, _SEMS)

        @block.vector
        def _(E):
            kb.emit("dve", E, _SEMS)

        @block.gpsimd
        def _(E):
            kb.emit("pool", E, _SEMS)

        @block.sync
        def _(E):
            kb.emit("sp", E, _SEMS)
    ctx.close()
    return nc


_SEMS = {}


def _host_inputs(inputs):
    g = {}
    g["wqt"] = np.ascontiguousarray(np.asarray(inputs["L0_attn_Wq"]).T)
    g["wkt"] = np.ascontiguousarray(np.asarray(inputs["L0_attn_Wk"]).T)
    g["wvt"] = np.ascontiguousarray(np.asarray(inputs["L0_attn_Wv"]).T)
    g["wot"] = np.ascontiguousarray(np.asarray(inputs["L0_attn_Wo"]).T)
    g["w1t"] = np.ascontiguousarray(np.asarray(inputs["L0_ffn_w1"]).T)
    g["w2t"] = np.ascontiguousarray(np.asarray(inputs["L0_ffn_w2"]).T)
    rwt = np.ascontiguousarray(np.asarray(inputs["L0_ffn_router_w"]).T)
    g["rwin"] = np.ascontiguousarray(
        rwt.reshape(8, 128, 16).transpose(1, 0, 2).reshape(128, 128))
    g["b1"] = np.asarray(inputs["L0_ffn_b1"]).reshape(1, -1)
    g["b2"] = np.asarray(inputs["L0_ffn_b2"]).reshape(1, -1)
    g["bq"] = np.asarray(inputs["L0_attn_bq"]).reshape(1, -1)
    g["bk"] = np.asarray(inputs["L0_attn_bk"]).reshape(1, -1)
    g["bv"] = np.asarray(inputs["L0_attn_bv"]).reshape(1, -1)
    g["bo"] = np.asarray(inputs["L0_attn_bo"]).reshape(1, -1)
    g["rb"] = np.asarray(inputs["L0_ffn_router_b"]).reshape(1, -1)
    g["idb"] = np.eye(128, dtype=np.float32).astype(ml_dtypes.bfloat16)
    g["idf"] = np.eye(128, dtype=np.float32)
    g["maskf"] = np.triu(np.ones((128, 128), dtype=np.float32))
    g["onescol"] = np.ones((128, 1), dtype=np.float32).astype(ml_dtypes.bfloat16)
    return g


def kernel(**inputs):
    x = np.asarray(inputs["L0_attn_x"])
    shared = _host_inputs(inputs)
    nc = _build()
    in_maps = []
    for c in range(8):
        m = dict(shared)
        m["x"] = np.ascontiguousarray(x[2 * c:2 * c + 2].reshape(4096, D))
        in_maps.append(m)
    res = run_bass_kernel_spmd(nc, in_maps, core_ids=list(range(8)))
    out = np.concatenate(
        [res.results[c]["out"].reshape(2, 2048, D) for c in range(8)], axis=0)
    return out


# version 20 (final)
# speedup vs baseline: 1.0095x; 1.0095x over previous
import numpy as np
import ml_dtypes
from contextlib import ExitStack

import concourse.bass as bass
import concourse.mybir as mybir
from concourse.bass_utils import run_bass_kernel_spmd

F32 = mybir.dt.float32
BF16 = mybir.dt.bfloat16
AF = mybir.ActivationFunctionType
ALU = mybir.AluOpType
AX = mybir.AxisListType

D = 1024
HF = 4096
NE = 16
NH = 16
DH = 64
NT = 8
NSET = 4
R2 = 32768
EPS = 1e-5

DO_FFN1 = True
DO_ATTN = True
DO_FFN2 = True
DEBUG_TAPS = False
ATTN_PART = "full"

ENGS = ("pe", "act", "dve", "pool", "sp")


def _cp(E, d, s):
    return E.tensor_copy(d, s) if hasattr(E, "tensor_copy") else E.copy(d, s)


def _scale(E, d, s, c):
    if hasattr(E, "tensor_scalar"):
        return E.tensor_scalar(d, s, c, None, ALU.mult)
    return E.mul(d, s, c)


class _Reg:
    __slots__ = ("writer", "readers", "name")

    def __init__(self, name=""):
        self.writer = None
        self.readers = []
        self.name = name


NDQ = 8


class _Op:
    __slots__ = ("eng", "fn", "deps", "rdeps", "pdeps", "need_inc", "inc_no",
                 "dma_val", "self_wait")

    def __init__(self, eng, fn):
        self.eng = eng
        self.fn = fn
        self.deps = []
        self.rdeps = []
        self.pdeps = []
        self.need_inc = False
        self.inc_no = None
        self.dma_val = None
        self.self_wait = None


class _KB:
    def __init__(self):
        self.eops = {e: [] for e in ENGS}
        self.pending = {e: [] for e in ENGS}
        self.dma_count = 0
        self.dq_uses = [0] * NDQ

    def op(self, eng, fn, reads=(), writes=(), ptr=()):
        o = _Op(eng, fn)
        for r in reads:
            if r.writer is not None:
                o.rdeps.append(r.writer)
        for r in ptr:
            if r.writer is not None:
                o.pdeps.append(r.writer)
        for w in writes:
            if w.writer is not None:
                o.deps.append(w.writer)
            o.deps.extend(w.readers)
        o.deps.extend(self.pending[eng])
        self.pending[eng] = []
        for r in reads:
            r.readers.append(o)
        for r in ptr:
            r.readers.append(o)
        for w in writes:
            w.writer = o
            w.readers = []
        self.eops[eng].append(o)
        return o

    def dma(self, fn, reads=(), writes=(), row=False):
        o = self.op("sp", fn, reads=reads, writes=writes)
        j = self.dma_count % NDQ
        self.dma_count += 1
        u = self.dq_uses[j]
        if u > 0:
            o.self_wait = (f"dq{j}", 16 * u)
        self.dq_uses[j] = u + 1
        o.dma_val = (f"dq{j}", 16 * (u + 1))
        return o

    def barrier(self):
        last = {e: (self.eops[e][-1] if self.eops[e] else None) for e in ENGS}
        for e in ENGS:
            deps = [last[o] for o in ENGS if o != e and last[o] is not None]
            self.pending[e].extend(deps)

    def finalize(self):
        for e in ENGS:
            for o in self.eops[e]:
                for d in o.deps + o.rdeps + o.pdeps:
                    if d.dma_val is None and not (d.eng == o.eng == "pe"):
                        d.need_inc = True
        for e in ENGS:
            n = 0
            for o in self.eops[e]:
                if o.need_inc:
                    n += 1
                    o.inc_no = n

    def emit(self, eng_name, E, sems):
        waited = {}
        my_sem = sems.get(eng_name)
        for o in self.eops[eng_name]:
            wl = {}
            for d in o.deps + o.rdeps + o.pdeps:
                if d.dma_val is not None:
                    sname, val = d.dma_val
                    wl[sname] = max(wl.get(sname, 0), val)
                else:
                    if d.eng == eng_name == "pe":
                        continue
                    assert d.inc_no is not None
                    wl[d.eng] = max(wl.get(d.eng, 0), d.inc_no)
            if o.self_wait is not None:
                sname, val = o.self_wait
                wl[sname] = max(wl.get(sname, 0), val)
            for sname, val in wl.items():
                if waited.get(sname, 0) >= val:
                    continue
                E.wait_ge(sems[sname], val)
                waited[sname] = val
            inst = o.fn(E)
            if o.dma_val is not None:
                inst.then_inc(sems[o.dma_val[0]], 16)
            elif o.need_inc:
                inst.then_inc(my_sem, 1)


def _build():
    nc = bass.Bass("TRN2", target_bir_lowering=False, debug=False)
    kb = _KB()

    dx = nc.dram_tensor("x", [4096, D], F32, kind="ExternalInput").ap()
    dout = nc.dram_tensor("out", [4096, D], F32, kind="ExternalOutput").ap()
    dwt = {}
    for nm in ("wqt", "wkt", "wvt", "wot"):
        dwt[nm] = nc.dram_tensor(nm, [D, D], F32, kind="ExternalInput").ap()
    dw1t = nc.dram_tensor("w1t", [D, HF], F32, kind="ExternalInput").ap()
    dw2t = nc.dram_tensor("w2t", [HF, D], F32, kind="ExternalInput").ap()
    drw = nc.dram_tensor("rwin", [128, 8 * NE], F32, kind="ExternalInput").ap()
    dbias = {}
    for nm, n in (("b1", HF), ("b2", D), ("bq", D), ("bk", D), ("bv", D),
                  ("bo", D), ("rb", NE)):
        dbias[nm] = nc.dram_tensor(nm, [1, n], F32, kind="ExternalInput").ap()
    didb = nc.dram_tensor("idb", [128, 128], BF16, kind="ExternalInput").ap()
    didf = nc.dram_tensor("idf", [128, 128], F32, kind="ExternalInput").ap()
    dmask = nc.dram_tensor("maskf", [128, 128], F32, kind="ExternalInput").ap()
    donescol = nc.dram_tensor("onescol", [128, 1], BF16, kind="ExternalInput").ap()
    dscr_w2 = nc.dram_tensor("scr_w2", [128, R2], BF16, kind="Internal").ap()
    dscr_at = nc.dram_tensor("scr_at", [128, R2], BF16, kind="Internal").ap()
    if DEBUG_TAPS:
        ddbg = nc.dram_tensor("dbg", [128, 64], F32, kind="ExternalOutput").ap()
        ddbgb = nc.dram_tensor("dbgb", [128, 2048], BF16,
                               kind="ExternalOutput").ap()

    ctx = ExitStack()
    sb = lambda n, s, dt: ctx.enter_context(nc.sbuf_tensor(n, s, dt))
    psa = lambda n, s, dt: nc.psum_tensor(n, s, dt)

    X = sb("X", [128, NT, D], F32)
    x_reg = [_Reg(f"x{t}") for t in range(NT)]
    arena = sb("arena", [128, 65536], BF16)
    w1_reg = [_Reg(f"w1_{k}") for k in range(8)]
    r2_reg = [_Reg(f"r2_{j}") for j in range(4)]
    kv_sb = sb("kv_sb", [128, 520], F32)
    kv_bf = [sb("kv_bf0", [128, 520], BF16), sb("kv_bf1", [128, 520], BF16)]
    kv_sb_r = _Reg("kv_sb")
    kv_bf_r = [_Reg("kv_bf0"), _Reg("kv_bf1")]
    sums = [sb(f"sums{i}", [128, NT], F32) for i in range(3)]
    sums_r = [_Reg(f"sums{i}") for i in range(3)]
    ssq = sb("ssq", [128, NT], F32)
    ssq_r = _Reg("ssq")
    sqj_g = sb("sqj_g", [128, 1024], F32)
    sqj_g_r = _Reg("sqj_g")
    mtmp = sb("mtmp", [128, NT], F32)
    vtmp = sb("vtmp", [128, NT], F32)
    ltmp = sb("ltmp", [128, NT], F32)
    rstd = sb("rstd", [128, NT], F32)
    nmr = sb("nmr", [128, NT], F32)
    st_r = {"m": _Reg("m"), "v": _Reg("v"), "l": _Reg("l"), "r": _Reg("r"), "n": _Reg("n")}
    rw = sb("rw_sb", [128, 8 * NE], F32)
    rw_r = _Reg("rw")
    brow = sb("brow", [128, HF + 128], BF16)
    brow_r = _Reg("brow")
    frow = sb("frow", [128, 544], F32)
    frow_r = _Reg("frow")
    epscol = sb("epscol", [128, 1], F32)
    idb = sb("idb_sb", [128, 128], BF16)
    idf = sb("idf_sb", [128, 128], F32)
    maskf = sb("maskf_sb", [128, 128], F32)
    onescol = sb("onescol_sb", [128, 1], BF16)
    const_r = _Reg("const")

    b1row = lambda lo, n: brow[0:1, lo:lo + n]
    bqrow = lambda lo, n: brow[32:33, lo:lo + n]
    bkrow = lambda lo, n: brow[32:33, 1024 + lo:1024 + lo + n]
    bvrow = lambda lo, n: brow[32:33, 2048 + lo:2048 + lo + n]
    b2row = lambda lo, n: brow[32:33, 3072 + lo:3072 + lo + n]
    borow = lambda lo, n: brow[64:65, lo:lo + n]
    ones0 = lambda n: brow[0:1, HF:HF + n]
    ones32 = lambda n: brow[32:33, HF:HF + n]
    ones64 = lambda n: brow[64:65, HF:HF + n]
    onesf = lambda n: frow[0:1, 0:n]
    rbrow = frow[0:1, 512:512 + NE]

    def set_rows(s):
        return (s // 2) * 2048 + (s % 2) * 1024

    def w1v(kc, lo, n):
        return arena[:, kc * HF + lo: kc * HF + lo + n]

    def w2v(hc, lo, n):
        return arena[:, R2 + hc * D + lo: R2 + hc * D + lo + n]

    def atv(m, kc, lo, n):
        base = R2 + m * 8192 + kc * D
        return arena[:, base + lo: base + lo + n]

    def dmaop(dst, src, reads=(), writes=(), row=False, slow=False):
        return kb.dma(lambda E, d=dst, s=src, sl=slow: E.dma_start(
            out=d, in_=s, allow_slow_non_contiguous=sl),
            reads=reads, writes=writes, row=row)

    dmaop(idb[:], didb, writes=[const_r])
    dmaop(idf[:], didf, writes=[const_r])
    dmaop(maskf[:], dmask, writes=[const_r])
    dmaop(onescol[:], donescol, writes=[const_r])
    dmaop(rw[:], drw, writes=[rw_r])

    astg = sb("astg", [128, 1024], F32)
    astg_r = _Reg("astg")
    abounce = sb("abounce", [128, 1024], BF16)
    abounce_r = _Reg("abounce")
    scr_at_r = _Reg("scr_at")

    p0ctx = ExitStack()
    stg = [p0ctx.enter_context(nc.sbuf_tensor(f"stg{i}", [128, 2048], F32))
           for i in range(2)]
    stg_r = [_Reg(f"stg{i}") for i in range(2)]

    cast_engine_rr = ["dve", "act", "pool"]

    def load_bias(name, dstap, n, scale=None):
        srcs = dbias[name]
        for piece in range(0, n, 2048):
            w = min(2048, n - piece)
            dmaop(stg[0][0:1, 0:w], srcs[0:1, piece:piece + w],
                  writes=[stg_r[0]], row=True)
            if scale is None:
                kb.op("dve", lambda E, d=dstap(piece, w), s=stg[0][0:1, 0:w]:
                      E.tensor_copy(d, s), reads=[stg_r[0]], writes=[brow_r])
            else:
                kb.op("dve", lambda E, d=dstap(piece, w), s=stg[0][0:1, 0:w],
                      sc=scale: E.tensor_scalar(d, s, sc, None, ALU.mult),
                      reads=[stg_r[0]], writes=[brow_r])

    load_bias("b1", b1row, HF)
    load_bias("bq", bqrow, D)
    load_bias("bk", bkrow, D)
    load_bias("bv", bvrow, D)
    load_bias("b2", b2row, D, scale=0.5)
    load_bias("bo", borow, D)
    for _ob in (ones0, ones32, ones64):
        kb.op("dve", lambda E, o=_ob(128): E.memset(o, 1.0), writes=[brow_r])
    kb.op("dve", lambda E: E.memset(epscol[:], EPS), writes=[const_r])
    kb.op("dve", lambda E: E.memset(onesf(512), 1.0), writes=[frow_r])
    dmaop(rbrow, dbias["rb"], writes=[frow_r], row=True)

    ce = 0
    for kc in range(8):
        for h2 in range(2):
            dst = stg[(kc * 2 + h2) % 2]
            dr = stg_r[(kc * 2 + h2) % 2]
            dmaop(dst[:], dw1t[kc * 128:(kc + 1) * 128, h2 * 2048:(h2 + 1) * 2048],
                  writes=[dr])
            eng = cast_engine_rr[ce % 3]; ce += 1
            kb.op(eng, lambda E, d=w1v(kc, h2 * 2048, 2048), s=dst[:]:
                  _cp(E, d, s), reads=[dr], writes=[w1_reg[kc]])
    for kc in range(8):
        for h2 in range(2):
            dst = stg[(kc * 2 + h2) % 2]
            dr = stg_r[(kc * 2 + h2) % 2]
            dmaop(dst[:].rearrange("p (b d) -> p b d", b=2),
                  dw2t[kc * 512 + h2 * 256:(kc * 512) + (h2 + 1) * 256, :]
                  .rearrange("(b a) d -> a b d", b=2),
                  writes=[dr])
            hc0 = kc * 4 + h2 * 2
            eng = ("dve", "act")[ce % 2]; ce += 1
            kb.op(eng, lambda E, d=w2v(hc0, 0, 2048), s=dst[:]:
                  _scale(E, d, s, 0.5),
                  reads=[dr], writes=[r2_reg[hc0 // 8]])

    scr_w2_r = _Reg("scr_w2")

    def emit_w2_spill():
        for j in range(4):
            dmaop(dscr_w2[:, j * 8192:(j + 1) * 8192],
                  arena[:, R2 + j * 8192: R2 + (j + 1) * 8192],
                  reads=[r2_reg[j]], writes=[scr_w2_r])

    def emit_attn_cast_to_scratch():
        for m, nm in enumerate(("wqt", "wkt", "wvt", "wot")):
            for kc in range(8):
                dmaop(astg[:], dwt[nm][kc * 128:(kc + 1) * 128, :],
                      writes=[astg_r])
                kb.op("pool", lambda E: _cp(E, abounce[:], astg[:]),
                      reads=[astg_r], writes=[abounce_r])
                dmaop(dscr_at[:, m * 8192 + kc * 1024: m * 8192 + (kc + 1) * 1024],
                      abounce[:], reads=[abounce_r], writes=[scr_at_r])

    def emit_attn_prep():
        for m in range(4):
            dmaop(arena[:, R2 + m * 8192:R2 + (m + 1) * 8192],
                  dscr_at[:, m * 8192:(m + 1) * 8192],
                  reads=[scr_at_r], writes=[r2_reg[m]])

    def emit_w2_reload():
        for j in range(4):
            dmaop(arena[:, R2 + j * 8192:R2 + (j + 1) * 8192],
                  dscr_w2[:, j * 8192:(j + 1) * 8192],
                  reads=[scr_w2_r], writes=[r2_reg[j]])

    def emit_sq(t):
        kb.op("act", lambda E, t=t: E.activation(
            sqj_g[:], X[:, t, :], AF.Square, accum_out=ssq[:, t:t + 1]),
            reads=[x_reg[t]], writes=[ssq_r, sqj_g_r])

    def load_tile_dma(s, t):
        r0 = set_rows(s)
        dmaop(X[:, t, :], dx[r0 + t * 128: r0 + (t + 1) * 128, :],
              writes=[x_reg[t]])

    def load_tile_stats(t):
        kb.op("dve", lambda E, t=t: E.reduce_sum(
            sums[0][:, t:t + 1], X[:, t, :], axis=AX.X),
            reads=[x_reg[t]], writes=[sums_r[0]])
        emit_sq(t)

    def load_set(s):
        for t in range(NT):
            load_tile_dma(s, t)
        for t in range(NT):
            load_tile_stats(t)

    load_set(0)
    p0ctx.close()

    def stats_chain(sum_reg, sums_t):
        kb.op("dve", lambda E: E.tensor_scalar(
            mtmp[:], sums_t[:], 1.0 / D, None, ALU.mult),
            reads=[sum_reg], writes=[st_r["m"]])
        kb.op("dve", lambda E: E.scalar_tensor_tensor(
            vtmp[:], mtmp[:], -1.0, mtmp[:], ALU.mult, ALU.mult),
            reads=[st_r["m"]], writes=[st_r["v"]])
        kb.op("dve", lambda E: E.tensor_scalar(
            ltmp[:], ssq[:], 1.0 / D, EPS, ALU.mult, ALU.add),
            reads=[ssq_r], writes=[st_r["l"]])
        kb.op("dve", lambda E: E.tensor_tensor(
            vtmp[:], ltmp[:], vtmp[:], ALU.add),
            reads=[st_r["l"], st_r["v"]], writes=[st_r["v"]])
        kb.op("act", lambda E: E.activation(
            ltmp[:], vtmp[:], AF.Ln, bias=0.0, scale=1.0),
            reads=[st_r["v"]], writes=[st_r["l"]])
        kb.op("act", lambda E: E.activation(
            rstd[:], ltmp[:], AF.Exp, bias=0.0, scale=-0.5),
            reads=[st_r["l"]], writes=[st_r["r"]])
        kb.op("dve", lambda E: E.scalar_tensor_tensor(
            nmr[:], mtmp[:], -1.0, rstd[:], ALU.mult, ALU.mult),
            reads=[st_r["m"], st_r["r"]], writes=[st_r["n"]])


    def emit_apply(t, lnb_tm, lnb_r, f32_dst=None, f32_reg=None):
        kb.op("dve", lambda E, t=t: E.tensor_scalar(
            lnb_tm[:], X[:, t, :], rstd[:, t:t + 1], nmr[:, t:t + 1],
            ALU.mult, ALU.add),
            reads=[x_reg[t]], writes=[lnb_r], ptr=[st_r["r"], st_r["n"]])
        if f32_dst is not None:
            kb.op("dve", lambda E, t=t: E.tensor_scalar(
                f32_dst[:], X[:, t, :], rstd[:, t:t + 1], nmr[:, t:t + 1],
                ALU.mult, ALU.add),
                reads=[x_reg[t]], writes=[f32_reg], ptr=[st_r["r"], st_r["n"]])

    def transpose8(src, src_reg, pdst, pdst_reg, ident, n=8):
        for j in range(n):
            kb.op("pe", lambda E, j=j: E.transpose(
                pdst[:, j * 128:(j + 1) * 128], src[:, j * 128:(j + 1) * 128],
                ident[:]),
                reads=[src_reg, const_r], writes=[pdst_reg])

    def emit_ffn_set(P, s, role, sums_idx, out_sums_idx, after_tile=None,
                     pre_tail=None):
        stats_chain(sums_r[sums_idx], sums[sums_idx])

        def tbf_block(t):
            par = t % 2
            transpose8(P["lnb_tm"][par], P["lnb_tm_r"][par], P["pTbf"],
                       P["pTbf_r"], idb)
            kb.op("dve", lambda E, par=par: E.tensor_copy(
                P["lnb_fm"][par][:], P["pTbf"][:]),
                reads=[P["pTbf_r"]], writes=[P["lnb_fm_r"][par]])

        def router_a(t):
            transpose8(P["lnf_tm"], P["lnf_tm_r"], P["pTf32"], P["pTf32_r"], idf)
            kb.op("act", lambda E: E.copy(P["lnf_fm"][:], P["pTf32"][:]),
                  reads=[P["pTf32_r"]], writes=[P["lnf_fm_r"]])

        def router_b(t):
            par = t % 2
            kb.op("pe", lambda E: E.matmul(
                P["pTf32"][:, 0:NE], onesf(128), rbrow, start=True, stop=False),
                reads=[frow_r], writes=[P["pTf32_r"]])
            for kc in range(8):
                kb.op("pe", lambda E, kc=kc: E.matmul(
                    P["pTf32"][:, 0:NE],
                    P["lnf_fm"][:, kc * 128:(kc + 1) * 128],
                    rw[:, kc * NE:(kc + 1) * NE],
                    start=False, stop=(kc == 7)),
                    reads=[P["lnf_fm_r"], rw_r], writes=[P["pTf32_r"]])
            kb.op("dve", lambda E, par=par: E.tensor_copy(
                P["logits"][par][:], P["pTf32"][:, 0:NE]),
                reads=[P["pTf32_r"]], writes=[P["logits_r"][par]])
            kb.op("dve", lambda E, par=par: E.max(
                out=P["gmax"][par][:], in_=P["logits"][par][:]),
                reads=[P["logits_r"][par]], writes=[P["gmax_r"][par]])
            kb.op("dve", lambda E, par=par: E.tensor_scalar(
                P["g"][par][:], P["logits"][par][:],
                P["gmax"][par][:, 3:4], None, ALU.is_ge),
                reads=[P["logits_r"][par]], writes=[P["g_r"][par]],
                ptr=[P["gmax_r"][par]])

        def router_block(t):
            router_a(t)
            router_b(t)

        emit_apply(0, P["lnb_tm"][0], P["lnb_tm_r"][0], P["lnf_tm"],
                   P["lnf_tm_r"])
        router_block(0)
        tbf_block(0)
        for t in range(NT):
            par = t % 2
            if t + 1 < NT:
                emit_apply(t + 1, P["lnb_tm"][(t + 1) % 2],
                           P["lnb_tm_r"][(t + 1) % 2], P["lnf_tm"],
                           P["lnf_tm_r"])
            for half in range(2):
                kb.op("pe", lambda E, half=half: E.matmul(
                    P["pY"][:, half * 512:(half + 1) * 512],
                    ones32(128), b2row(half * 512, 512),
                    start=True, stop=False),
                    reads=[brow_r], writes=[P["pY_r"]])

            def h_group(oc):
                b = oc % 2
                kb.op("pe", lambda E, oc=oc, b=b: E.matmul(
                    P["pH"][:, b * 512:(b + 1) * 512],
                    ones0(128), b1row(oc * 512, 512), start=True, stop=False),
                    reads=[brow_r], writes=[P["pH_r"][b]])
                for kc in range(8):
                    kb.op("pe", lambda E, oc=oc, b=b, kc=kc, par=par: E.matmul(
                        P["pH"][:, b * 512:(b + 1) * 512],
                        P["lnb_fm"][par][:, kc * 128:(kc + 1) * 128],
                        w1v(kc, oc * 512, 512),
                        start=False, stop=(kc == 7)),
                        reads=[P["lnb_fm_r"][par], w1_reg[kc], brow_r],
                        writes=[P["pH_r"][b]])

            def gate(oc):
                b = oc % 2
                gb = P["g"][par][:, 2 * oc:2 * oc + 2].unsqueeze(2).to_broadcast(
                    [128, 2, 256])
                kb.op("dve", lambda E, oc=oc, b=b, gb=gb: E.scalar_tensor_tensor(
                    P["hg_tm"][b][:].rearrange("p (a c) -> p a c", a=2),
                    P["pH"][:, b * 512:(b + 1) * 512].rearrange(
                        "p (a c) -> p a c", a=2),
                    0.0, gb, ALU.max, ALU.mult),
                    reads=[P["pH_r"][b], P["g_r"][par]],
                    writes=[P["hg_tm_r"][b]])

            def hgT(oc):
                b = oc % 2
                for j in range(4):
                    kb.op("pe", lambda E, b=b, j=j: E.transpose(
                        P["pHT"][:, b * 512 + j * 128: b * 512 + (j + 1) * 128],
                        P["hg_tm"][b][:, j * 128:(j + 1) * 128], idb[:]),
                        reads=[P["hg_tm_r"][b], const_r], writes=[P["pHT_r"][b]])
                kb.op("act", lambda E, b=b: E.copy(
                    P["hg_fm"][b][:], P["pHT"][:, b * 512:(b + 1) * 512]),
                    reads=[P["pHT_r"][b]], writes=[P["hg_fm_r"][b]])

            def y_mms(oc):
                b = oc % 2
                for j in range(4):
                    hc = oc * 4 + j
                    for half in range(2):
                        kb.op("pe", lambda E, b=b, j=j, hc=hc, half=half, oc=oc:
                              E.matmul(
                                  P["pY"][:, half * 512:(half + 1) * 512],
                                  P["hg_fm"][b][:, j * 128:(j + 1) * 128],
                                  w2v(hc, half * 512, 512),
                                  start=False,
                                  stop=(oc == 7 and j == 3)),
                              reads=[P["hg_fm_r"][b], r2_reg[hc // 8]],
                              writes=[P["pY_r"]])

            h_group(0)
            h_group(1)
            for oc in range(8):
                gate(oc)
                hgT(oc)
                if oc == 1 and t + 1 < NT:
                    router_a(t + 1)
                if oc == 3 and t + 1 < NT:
                    router_b(t + 1)
                if oc == 4 and t + 1 < NT:
                    tbf_block(t + 1)
                if oc + 2 < 8:
                    h_group(oc + 2)
                if oc >= 1:
                    y_mms(oc - 1)
            y_mms(7)
            if pre_tail is not None and t == NT - 1:
                pre_tail()
            if out_sums_idx is not None:
                kb.op("dve", lambda E, t=t: E.scalar_tensor_tensor(
                    X[:, t, :], P["pY"][:], 1.0, X[:, t, :], ALU.mult, ALU.add,
                    accum_out=sums[out_sums_idx][:, t:t + 1]),
                    reads=[P["pY_r"], x_reg[t]],
                    writes=[x_reg[t], sums_r[out_sums_idx]])
                emit_sq(t)
            else:
                kb.op("dve", lambda E, t=t: E.scalar_tensor_tensor(
                    X[:, t, :], P["pY"][:], 1.0, X[:, t, :], ALU.mult, ALU.add),
                    reads=[P["pY_r"], x_reg[t]], writes=[x_reg[t]])
            if role == "f2":
                r0 = set_rows(s)
                dmaop(dout[r0 + t * 128:r0 + (t + 1) * 128, :], X[:, t, :],
                      reads=[x_reg[t]])
            if after_tile is not None:
                after_tile(t)

    def emit_out_only(s):
        r0 = set_rows(s)
        for t in range(NT):
            dmaop(dout[r0 + t * 128:r0 + (t + 1) * 128, :], X[:, t, :],
                  reads=[x_reg[t]])

    def ffn_phase(pi, f2_set, f1_set, f2_sums, f1_out_sums, f1_load=True,
                  prep_attn=None):
        with ExitStack() as ph:
            psb = lambda n, s, dt: ph.enter_context(
                nc.sbuf_tensor(f"{n}_p{pi}", s, dt))
            pps = lambda n, s, dt: ph.enter_context(
                nc.psum_tensor(f"{n}_p{pi}", s, dt))
            P = {}
            P["pTbf"] = pps("pTbf", [128, 1024], BF16); P["pTbf_r"] = _Reg()
            P["pTf32"] = pps("pTf32", [128, 1024], F32); P["pTf32_r"] = _Reg()
            P["pH"] = pps("pH", [128, 1024], F32)
            P["pH_r"] = [_Reg(), _Reg()]
            P["pY"] = pps("pY", [128, 1024], F32); P["pY_r"] = _Reg()
            P["pHT"] = pps("pHT", [128, 1024], BF16)
            P["pHT_r"] = [_Reg(), _Reg()]
            P["lnb_tm"] = [psb(f"lnbtm{i}", [128, 1024], BF16) for i in range(2)]
            P["lnb_tm_r"] = [_Reg(), _Reg()]
            P["lnb_fm"] = [psb(f"lnbfm{i}", [128, 1024], BF16) for i in range(2)]
            P["lnb_fm_r"] = [_Reg(), _Reg()]
            P["lnf_tm"] = psb("lnftm", [128, 1024], F32); P["lnf_tm_r"] = _Reg()
            P["lnf_fm"] = psb("lnffm", [128, 1024], F32); P["lnf_fm_r"] = _Reg()
            P["logits"] = [psb(f"lg{i}", [128, NE], F32) for i in range(2)]
            P["logits_r"] = [_Reg(), _Reg()]
            P["gmax"] = [psb(f"gm{i}", [128, 8], F32) for i in range(2)]
            P["gmax_r"] = [_Reg(), _Reg()]
            P["g"] = [psb(f"g{i}", [128, NE], F32) for i in range(2)]
            P["g_r"] = [_Reg(), _Reg()]
            P["hg_tm"] = [psb(f"hgtm{i}", [128, 512], BF16) for i in range(2)]
            P["hg_tm_r"] = [_Reg(), _Reg()]
            P["hg_fm"] = [psb(f"hgfm{i}", [128, 512], BF16) for i in range(2)]
            P["hg_fm_r"] = [_Reg(), _Reg()]

            pt = None
            if prep_attn is not None and DO_ATTN:
                pt = emit_attn_prep
            cb = None
            if f1_set is not None and f1_load:
                def cb(t, fs=f1_set):
                    load_tile_dma(fs, t)
                    if t >= 2:
                        load_tile_stats(t - 2)
                    if t == NT - 1:
                        load_tile_stats(NT - 2)
                        load_tile_stats(NT - 1)
            last_is_f1 = f1_set is not None and DO_FFN1
            if f2_set is not None:
                if DO_FFN2:
                    emit_ffn_set(P, f2_set, "f2", f2_sums, None, after_tile=cb,
                                 pre_tail=None if last_is_f1 else pt)
                    cb = None
                else:
                    emit_out_only(f2_set)
                    if not last_is_f1 and pt is not None:
                        pt()
            if f1_set is not None:
                if cb is not None:
                    load_set(f1_set)
                if DO_FFN1:
                    emit_ffn_set(P, f1_set, "f1", 0, f1_out_sums, pre_tail=pt)
        kb.barrier()

    def attn_phase(pi, s, first, reset):
        with ExitStack() as ph:
            psb = lambda n, sh, dt: ph.enter_context(
                nc.sbuf_tensor(f"{n}_a{pi}", sh, dt))
            pps = lambda n, sh, dt: ph.enter_context(
                nc.psum_tensor(f"{n}_a{pi}", sh, dt))
            pQ = pps("pQ", [128, 1024], F32)
            pQ_r = [_Reg("pQ0"), _Reg("pQ1")]
            pK = pps("pK", [128, 1024], F32)
            pK_r = [_Reg("pK0"), _Reg("pK1")]
            pV = pps("pV", [128, 1024], F32)
            pV_r = [_Reg("pV0"), _Reg("pV1")]
            pT1 = pps("pT1", [128, 1024], BF16); pT1_r = _Reg("pT1")
            pT2 = pps("pT2", [128, 1024], BF16); pT2_r = _Reg("pT2")
            lnb_tm = psb("lnbtm", [128, 1024], BF16); lnb_tm_r = _Reg()
            lnb_fm = psb("lnbfm", [128, 1024], BF16); lnb_fm_r = _Reg()
            scr, scr_r = sqj_g, sqj_g_r
            q_sp = psb("qsp", [128, 1024], BF16); q_sp_r = _Reg()
            k_sp = psb("ksp", [128, 1024], BF16); k_sp_r = _Reg()
            q_fm = psb("qfm", [128, 1024], BF16); q_fm_r = _Reg()
            k_fm = psb("kfm", [128, 1024], BF16); k_fm_r = _Reg()
            v_tm = psb("vtm", [128, 1024], BF16); v_tm_r = _Reg()
            am = [psb(f"am{i}", [128, 512], BF16) for i in range(2)]
            am_r = [_Reg(), _Reg()]
            att_tm = psb("atm", [128, 1024], BF16)
            att_tm_r = [_Reg(), _Reg()]
            att_fm = psb("afm", [128, 1024], BF16)
            att_fm_r = [_Reg(), _Reg()]
            r_all = psb("rall", [128, NE], F32); r_all_r = _Reg()

            if reset:
                kb.op("dve", lambda E: E.memset(kv_sb[:], 0.0), writes=[kv_sb_r])

            stats_chain(sums_r[1], sums[1])
            emit_apply(0, lnb_tm, lnb_tm_r)

            def ln2t_block():
                transpose8(lnb_tm, lnb_tm_r, pT1, pT1_r, idb)
                kb.op("dve", lambda E: E.tensor_copy(lnb_fm[:], pT1[:]),
                      reads=[pT1_r], writes=[lnb_fm_r])

            def proj_one(ps, ps_r, m, bias):
                for half in range(2):
                    kb.op("pe", lambda E, ps=ps, half=half, bias=bias:
                          E.matmul(ps[:, half * 512:(half + 1) * 512],
                                   ones32(128), bias(half * 512, 512),
                                   start=True, stop=False),
                          reads=[brow_r], writes=[ps_r[half]])
                    for kc in range(8):
                        kb.op("pe", lambda E, ps=ps, half=half, kc=kc, m=m:
                              E.matmul(ps[:, half * 512:(half + 1) * 512],
                                       lnb_fm[:, kc * 128:(kc + 1) * 128],
                                       atv(m, kc, half * 512, 512),
                                       start=False, stop=(kc == 7)),
                              reads=[lnb_fm_r, r2_reg[m], brow_r],
                              writes=[ps_r[half]])

            ln2t_block()
            proj_one(pQ, pQ_r, 0, bqrow)
            proj_one(pV, pV_r, 2, bvrow)
            for c in range(NT):
                skip_cross = reset and c == 0
                proj_one(pK, pK_r, 1, bkrow)
                kb.op("act", lambda E: E.activation(scr[:], pQ[:], AF.Exp),
                      reads=[pQ_r[0], pQ_r[1]], writes=[scr_r])
                kb.op("act", lambda E: E.activation(
                    q_sp[:], scr[:], AF.Ln, bias=1.0),
                    reads=[scr_r], writes=[q_sp_r])
                kb.op("act", lambda E: E.activation(scr[:], pK[:], AF.Exp),
                      reads=[pK_r[0], pK_r[1]], writes=[scr_r])
                kb.op("act", lambda E: E.activation(
                    k_sp[:], scr[:], AF.Ln, bias=1.0),
                    reads=[scr_r], writes=[k_sp_r])
                kb.op("dve", lambda E: E.tensor_copy(v_tm[:], pV[:]),
                      reads=[pV_r[0], pV_r[1]], writes=[v_tm_r])
                if c + 1 < NT:
                    emit_apply(c + 1, lnb_tm, lnb_tm_r)
                if ATTN_PART == "qkv":
                    continue
                transpose8(q_sp, q_sp_r, pT1, pT1_r, idb)
                kb.op("dve", lambda E: E.tensor_copy(q_fm[:], pT1[:]),
                      reads=[pT1_r], writes=[q_fm_r])
                transpose8(k_sp, k_sp_r, pT2, pT2_r, idb)
                kb.op("dve", lambda E: E.tensor_copy(k_fm[:], pT2[:]),
                      reads=[pT2_r], writes=[k_fm_r])
                for h in range(NH):
                    kdst = pV[(h % 2) * 64:(h % 2) * 64 + 64,
                              (h // 2) * 64:(h // 2) * 64 + 64]
                    kb.op("pe", lambda E, h=h, kdst=kdst: E.matmul(
                        kdst, k_sp[:, h * 64:(h + 1) * 64],
                        v_tm[:, h * 64:(h + 1) * 64], start=True, stop=True),
                        reads=[k_sp_r, v_tm_r], writes=[pV_r[0]])
                    k1dst = pV[(h % 2) * 64:(h % 2) * 64 + 64,
                               512 + h // 2:513 + h // 2]
                    kb.op("pe", lambda E, h=h, k1dst=k1dst: E.matmul(
                        k1dst, k_sp[:, h * 64:(h + 1) * 64], onescol[:],
                        start=True, stop=True),
                        reads=[k_sp_r, const_r], writes=[pV_r[1]])

                def hsl(buf, h):
                    return buf[(h % 2) * 64:(h % 2) * 64 + 64,
                               (h // 2) * 128:(h // 2) * 128 + 128]

                def at_group(g):
                    gb2 = (g % 2) * 256
                    for j in range(4):
                        h = g * 4 + j
                        bank = h % 2
                        col = bank * 512 + gb2 + (j // 2) * 128
                        kb.op("pe", lambda E, h=h, col=col: E.matmul(
                            pQ[:, col:col + 128],
                            hsl(k_fm, h), hsl(q_fm, h), start=True, stop=True),
                            reads=[k_fm_r, q_fm_r], writes=[pQ_r[bank]])
                    mb = maskf[:].unsqueeze(1).to_broadcast([128, 2, 128])
                    amv = am[g % 2][:].rearrange("p (a b c) -> p a b c", a=2, b=2)
                    for par in range(2):
                        src = pQ[:, par * 512 + gb2: par * 512 + gb2 + 256]
                        kb.op("dve", lambda E, par=par, src=src, amv=amv,
                              mb=mb: E.tensor_tensor(
                            amv[:, :, par, :],
                            src.rearrange("p (a c) -> p a c", a=2),
                            mb, ALU.mult),
                            reads=[pQ_r[par], const_r], writes=[am_r[g % 2]])

                def numden(g):
                    for j in range(4):
                        h = g * 4 + j
                        amh = am[g % 2][:, j * 128:(j + 1) * 128]
                        nsl = pK[:, h * 64:(h + 1) * 64]
                        kb.op("pe", lambda E, amh=amh, nsl=nsl, h=h,
                              sc=skip_cross: E.matmul(
                            nsl, amh, v_tm[:, h * 64:(h + 1) * 64],
                            start=True, stop=sc),
                            reads=[am_r[g % 2], v_tm_r], writes=[pK_r[h // 8]])
                        if not skip_cross:
                            kvblk = kv_bf[c % 2][(h % 2) * 64:(h % 2) * 64 + 64,
                                                 (h // 2) * 64:(h // 2) * 64 + 64]
                            kb.op("pe", lambda E, nsl=nsl, h=h, kvblk=kvblk:
                                  E.matmul(nsl, hsl(q_fm, h), kvblk,
                                           start=False, stop=True),
                                  reads=[q_fm_r, kv_bf_r[c % 2]],
                                  writes=[pK_r[h // 8]])
                        dsl = pV[:, 528 + h:529 + h]
                        kb.op("pe", lambda E, amh=amh, dsl=dsl,
                              sc=skip_cross: E.matmul(
                            dsl, amh, onescol[:], start=True, stop=sc),
                            reads=[am_r[g % 2], const_r], writes=[pV_r[1]])
                        if not skip_cross:
                            k1c = kv_bf[c % 2][(h % 2) * 64:(h % 2) * 64 + 64,
                                               512 + h // 2:513 + h // 2]
                            kb.op("pe", lambda E, dsl=dsl, h=h, k1c=k1c:
                                  E.matmul(dsl, hsl(q_fm, h), k1c,
                                           start=False, stop=True),
                                  reads=[q_fm_r, kv_bf_r[c % 2]],
                                  writes=[pV_r[1]])

                if ATTN_PART == "trans":
                    continue
                if ATTN_PART == "atg":
                    at_group(0)
                    at_group(1)
                    at_group(2)
                    at_group(3)
                    continue
                at_group(0)
                at_group(1)
                numden(0)
                at_group(2)
                numden(1)
                at_group(3)
                numden(2)
                numden(3)
                if c + 1 < NT:
                    ln2t_block()
                    proj_one(pQ, pQ_r, 0, bqrow)
                if ATTN_PART == "numden":
                    continue
                kb.op("dve", lambda E: E.reciprocal(r_all[:], pV[:, 528:544]),
                      reads=[pV_r[1]], writes=[r_all_r])
                for hf in range(2):
                    rb_ = r_all[:, hf * 8:(hf + 1) * 8].unsqueeze(2)\
                        .to_broadcast([128, 8, DH])
                    sl = slice(hf * 512, (hf + 1) * 512)
                    kb.op("dve", lambda E, rb_=rb_, sl=sl: E.tensor_tensor(
                        att_tm[:, sl].rearrange("p (h e) -> p h e", h=8),
                        pK[:, sl].rearrange("p (h e) -> p h e", h=8),
                        rb_, ALU.mult),
                        reads=[pK_r[hf], r_all_r], writes=[att_tm_r[hf]])
                    for j in range(4):
                        jj = hf * 4 + j
                        kb.op("pe", lambda E, jj=jj: E.transpose(
                            pT2[:, jj * 128:(jj + 1) * 128],
                            att_tm[:, jj * 128:(jj + 1) * 128], idb[:]),
                            reads=[att_tm_r[hf], const_r], writes=[pT2_r])
                    kb.op("dve", lambda E, sl=sl: E.tensor_copy(
                        att_fm[:, sl], pT2[:, sl]),
                        reads=[pT2_r], writes=[att_fm_r[hf]])
                if ATTN_PART == "scan":
                    continue
                kb.op("dve", lambda E: E.tensor_tensor(
                    kv_sb[:], kv_sb[:], pV[:, 0:520], ALU.add),
                    reads=[pV_r[0], pV_r[1], kv_sb_r], writes=[kv_sb_r])
                kb.op("dve", lambda E, c=c: E.tensor_copy(
                    kv_bf[(c + 1) % 2][:], kv_sb[:]),
                    reads=[kv_sb_r], writes=[kv_bf_r[(c + 1) % 2]])
                for half in range(2):
                    kb.op("pe", lambda E, half=half: E.matmul(
                        pK[:, half * 512:(half + 1) * 512], ones64(128),
                        borow(half * 512, 512), start=True, stop=False),
                        reads=[brow_r], writes=[pK_r[half]])
                for kc in range(8):
                    for half in range(2):
                        kb.op("pe", lambda E, half=half, kc=kc: E.matmul(
                            pK[:, half * 512:(half + 1) * 512],
                            att_fm[:, kc * 128:(kc + 1) * 128],
                            atv(3, kc, half * 512, 512),
                            start=False, stop=(kc == 7)),
                            reads=[att_fm_r[kc // 4], r2_reg[3], brow_r],
                            writes=[pK_r[half]])
                if c + 1 < NT:
                    proj_one(pV, pV_r, 2, bvrow)
                kb.op("dve", lambda E, c=c: E.scalar_tensor_tensor(
                    X[:, c, :], pK[:], 1.0, X[:, c, :], ALU.mult, ALU.add,
                    accum_out=sums[2][:, c:c + 1]),
                    reads=[pK_r[0], pK_r[1], x_reg[c]],
                    writes=[x_reg[c], sums_r[2]])
                emit_sq(c)
                if ATTN_PART == "proj":
                    continue
            emit_w2_reload()
        kb.barrier()

    kb.barrier()
    emit_w2_spill()
    if DO_ATTN:
        emit_attn_cast_to_scratch()
    ffn_phase(1, None, 0, None, 1, f1_load=False, prep_attn="reload")
    for s in range(NSET):
        if DO_ATTN:
            attn_phase(2 + 2 * s, s, first=(s == 0), reset=(s % 2 == 0))
        f2_sums = 2 if DO_ATTN else 1
        if not DO_FFN1:
            f2_sums = 2 if DO_ATTN else 0
        ffn_phase(3 + 2 * s, s, s + 1 if s + 1 < NSET else None, f2_sums, 1,
                  prep_attn=("reload" if s + 1 < NSET else None))

    for j in range(NDQ):
        kb.op("sp", lambda E, j=j: E.wait_ge(_SEMS[f"dq{j}"],
                                             16 * kb.dq_uses[j]))

    kb.finalize()
    global _SEMS
    sem_names = ["pe", "act", "dve", "pool", "sp"] + [f"dq{j}" for j in range(NDQ)]
    sem_ctx = [ctx.enter_context(nc.semaphore(n)) for n in sem_names]
    _SEMS = dict(zip(sem_names, sem_ctx))
    with nc.Block() as block:
        @block.tensor
        def _(E):
            kb.emit("pe", E, _SEMS)

        @block.scalar
        def _(E):
            kb.emit("act", E, _SEMS)

        @block.vector
        def _(E):
            kb.emit("dve", E, _SEMS)

        @block.gpsimd
        def _(E):
            kb.emit("pool", E, _SEMS)

        @block.sync
        def _(E):
            kb.emit("sp", E, _SEMS)
    ctx.close()
    return nc


_SEMS = {}


def _host_inputs(inputs):
    g = {}
    g["wqt"] = np.ascontiguousarray(np.asarray(inputs["L0_attn_Wq"]).T)
    g["wkt"] = np.ascontiguousarray(np.asarray(inputs["L0_attn_Wk"]).T)
    g["wvt"] = np.ascontiguousarray(np.asarray(inputs["L0_attn_Wv"]).T)
    g["wot"] = np.ascontiguousarray(np.asarray(inputs["L0_attn_Wo"]).T)
    g["w1t"] = np.ascontiguousarray(np.asarray(inputs["L0_ffn_w1"]).T)
    g["w2t"] = np.ascontiguousarray(np.asarray(inputs["L0_ffn_w2"]).T)
    rwt = np.ascontiguousarray(np.asarray(inputs["L0_ffn_router_w"]).T)
    g["rwin"] = np.ascontiguousarray(
        rwt.reshape(8, 128, 16).transpose(1, 0, 2).reshape(128, 128))
    g["b1"] = np.asarray(inputs["L0_ffn_b1"]).reshape(1, -1)
    g["b2"] = np.asarray(inputs["L0_ffn_b2"]).reshape(1, -1)
    g["bq"] = np.asarray(inputs["L0_attn_bq"]).reshape(1, -1)
    g["bk"] = np.asarray(inputs["L0_attn_bk"]).reshape(1, -1)
    g["bv"] = np.asarray(inputs["L0_attn_bv"]).reshape(1, -1)
    g["bo"] = np.asarray(inputs["L0_attn_bo"]).reshape(1, -1)
    g["rb"] = np.asarray(inputs["L0_ffn_router_b"]).reshape(1, -1)
    g["idb"] = np.eye(128, dtype=np.float32).astype(ml_dtypes.bfloat16)
    g["idf"] = np.eye(128, dtype=np.float32)
    g["maskf"] = np.triu(np.ones((128, 128), dtype=np.float32))
    g["onescol"] = np.ones((128, 1), dtype=np.float32).astype(ml_dtypes.bfloat16)
    return g


def kernel(**inputs):
    x = np.asarray(inputs["L0_attn_x"])
    shared = _host_inputs(inputs)
    nc = _build()
    in_maps = []
    for c in range(8):
        m = dict(shared)
        m["x"] = np.ascontiguousarray(x[2 * c:2 * c + 2].reshape(4096, D))
        in_maps.append(m)
    res = run_bass_kernel_spmd(nc, in_maps, core_ids=list(range(8)))
    out = np.concatenate(
        [res.results[c]["out"].reshape(2, 2048, D) for c in range(8)], axis=0)
    return out
